# version 01 (final)
# baseline (speedup 1.0000x reference)
import numpy as np
import concourse.bass as bass
import concourse.mybir as mybir
from concourse.bass_utils import run_bass_kernel_spmd
from contextlib import ExitStack

F32 = mybir.dt.float32
U32 = mybir.dt.uint32
AF = mybir.ActivationFunctionType
ALU = mybir.AluOpType
AX = mybir.AxisListType

MAGIC = 12582912.0
QMAX = 32767.0
EPS = 1e-5

NCORES = 8
BPC = 2
S = 2048
D = 1024
HEADS = 16
DH = 64
PAIRS = 8
I3 = 3072
TPC = BPC * S
NTT = TPC // 128
IC = I3 // 128

ENGINES = ("tensor", "vector", "scalar", "gpsimd", "sync")


class Prog:

    def __init__(self, nc, ctx):
        self.nc = nc
        self.sems = {e: ctx.enter_context(nc.semaphore("sem_" + e)) for e in ENGINES}
        self.cnt = {e: 0 for e in ENGINES}
        self.actions = {e: [] for e in ENGINES}
        self.seen = {e: {} for e in ENGINES}
        self.last_w = {}
        self.readers = {}
        self.n_instr = 0

    def _wait(self, eng, e2, c):
        if c <= 0 or self.seen[eng].get(e2, 0) >= c:
            return
        self.seen[eng][e2] = c
        sem = self.sems[e2]
        self.actions[eng].append(lambda obj, s=sem, v=c: obj.wait_ge(s, v))
        self.n_instr += 1

    def op(self, eng, fn, reads=(), writes=(), dma=False):
        need = {}
        for k in reads:
            lw = self.last_w.get(k)
            if lw:
                need[lw[0]] = max(need.get(lw[0], 0), lw[1])
        for k in writes:
            lw = self.last_w.get(k)
            if lw:
                need[lw[0]] = max(need.get(lw[0], 0), lw[1])
            for e2, c in self.readers.get(k, {}).items():
                need[e2] = max(need.get(e2, 0), c)
        for e2, c in need.items():
            self._wait(eng, e2, c)
        inc = 16 if dma else 1
        sem = self.sems[eng]
        self.actions[eng].append(lambda obj, f=fn, s=sem, i=inc: f(obj).then_inc(s, i))
        self.n_instr += 1
        self.cnt[eng] += inc
        c = self.cnt[eng]
        for k in writes:
            self.last_w[k] = (eng, c)
            self.readers[k] = {}
        for k in reads:
            self.readers.setdefault(k, {})[eng] = c

    def barrier(self):
        for e in ENGINES:
            for e2 in ENGINES:
                if e2 != e:
                    self._wait(e, e2, self.cnt[e2])


def build_nc(stop=None):
    nc = bass.Bass("TRN2", target_bir_lowering=False, detect_race_conditions=False)

    x_h = nc.dram_tensor("x", [TPC, D], F32, kind="ExternalInput")
    wq_h = nc.dram_tensor("wq", [D, D], F32, kind="ExternalInput")
    wkv_h = nc.dram_tensor("wkv", [D, 2 * D], F32, kind="ExternalInput")
    wo_h = nc.dram_tensor("wo", [D, D], F32, kind="ExternalInput")
    bo_h = nc.dram_tensor("bo", [1, D], F32, kind="ExternalInput")
    wup_h = nc.dram_tensor("wup", [I3, D], F32, kind="ExternalInput")
    wgate_h = nc.dram_tensor("wgate", [I3, D], F32, kind="ExternalInput")
    wdown_h = nc.dram_tensor("wdown", [D, I3], F32, kind="ExternalInput")
    out_h = nc.dram_tensor("out", [TPC, D], F32, kind="ExternalOutput")

    x1_h = nc.dram_tensor("x1s", [TPC, D], F32)
    ln3_h = nc.dram_tensor("ln3s", [NTT, 8, 128, 128], F32)
    up_h = nc.dram_tensor("ups", [IC, 128, TPC], F32)
    gate_h = nc.dram_tensor("gates", [IC, 128, TPC], F32)
    cc_n = [1, 2, 1, 1]
    cc_in = [nc.dram_tensor(f"cc{i}i", [1, cc_n[i]], F32) for i in range(4)]
    cc_out = [nc.dram_tensor(f"cc{i}o", [1, cc_n[i]], F32) for i in range(4)]

    ident_h = nc.inline_tensor(np.eye(128, dtype=np.float32), "ident")
    ones_col_h = nc.inline_tensor(np.ones((1, 128), np.float32), "onescol")
    ones128_h = nc.inline_tensor(np.ones((128, 1), np.float32), "ones128")

    ctx = ExitStack()
    sb = lambda name, shape: ctx.enter_context(nc.sbuf_tensor(name, shape, F32))

    AR1 = sb("AR1", [128, 16384])
    AR2 = sb("AR2", [128, 16384])
    AR3 = sb("AR3", [128, 10240])
    x_t = sb("x_t", [128, 1024])
    h_t = sb("h_t", [128, 1024])
    x1_t = sb("x1_t", [128, 1024])
    ln3_t = sb("ln3_t", [128, 1024])
    stage = sb("stage", [128, 1024])
    ident = sb("ident_s", [128, 128])
    onec = sb("onec_s", [1, 128])
    one128 = sb("one128_s", [128, 1])
    bo_s = sb("bo_s", [1, 1024])
    st = sb("st", [128, 16])
    lnm = sb("lnm", [128, 1])
    lnu = sb("lnu", [128, 1])
    lnr = sb("lnr", [128, 1])
    lnt = sb("lnt", [128, 1])
    amax_t = sb("amax_t", [128, 1])
    m_ln3 = sb("m_ln3", [128, 1])
    m_up = sb("m_up", [128, 1])
    m_gate = sb("m_gate", [128, 1])
    m_gs = sb("m_gs", [128, 1])
    m_o = sb("m_o", [128, 1])
    sc_a = sb("sc_a", [1, 2])
    sc_b = sb("sc_b", [1, 2])
    r_row = sb("r_row", [1, 512])
    Rb = sb("Rb", [128, 512])
    sxb = sb("sxb", [128, 1]); invxb = sb("invxb", [128, 1])
    sub_ = sb("sub_", [128, 1]); invub = sb("invub", [128, 1])
    sgb = sb("sgb", [128, 1]); invgb = sb("invgb", [128, 1])
    sgsb = sb("sgsb", [128, 1]); invgsb = sb("invgsb", [128, 1])
    sob = sb("sob", [128, 1]); invob = sb("invob", [128, 1])
    wsc = sb("wsc", [128, 96])
    wrc = sb("wrc", [128, 96])

    ps = [ctx.enter_context(nc.psum_tensor(f"ps{i}", [128, 512], F32)) for i in range(8)]

    p = Prog(nc, ctx)

    def hT(c):
        return AR1[:, c * 2048:(c + 1) * 2048]

    def oT(pair):
        return AR2[:, pair * 2048:(pair + 1) * 2048]

    def xqk(c):
        if c < 4:
            return AR1[:, c * 4096:(c + 1) * 4096]
        return AR2[:, (c - 4) * 4096:(c - 3) * 4096]

    def wdT(i):
        if i < 16:
            return AR1[:, i * 1024:(i + 1) * 1024]
        return AR2[:, (i - 16) * 1024:(i - 15) * 1024]

    qT = AR3[:, 0:2048]
    kT = AR3[:, 2048:4096]
    V = AR3[:, 4096:6144]
    wsl_ = AR3[:, 6144:9216]
    expT = [AR3[:, 9216 + b * 512: 9216 + (b + 1) * 512] for b in range(2)]
    WoV = AR3[:, 0:8192]
    rowA = AR3[:, 0:4096]
    rowB = AR3[:, 4096:8192]
    wnat = AR3[:, 0:4096]
    wT = AR3[:, 4096:5120]
    upr = AR3[:, 5120:9216]
    kob = [AR3[:, 8192 + b * 512: 8192 + (b + 1) * 512] for b in range(2)]
    ffn_t = AR3[:, 4096:5120]
    outt = AR3[:, 5120:6144]

    def TS(eng, out, in0, s1, s2, op0, op1, reads, writes):
        if s2 is None:
            p.op(eng, lambda e, o=out, i=in0, a=s1, u=op0:
                 e.tensor_scalar(o, i, a, None, u), reads=reads, writes=writes)
        else:
            p.op(eng, lambda e, o=out, i=in0, a=s1, b=s2, u=op0, v=op1:
                 e.tensor_scalar(o, i, a, b, u, v), reads=reads, writes=writes)

    def TT(eng, out, in0, in1, alu, reads, writes):
        p.op(eng, lambda e, o=out, i=in0, j=in1, a=alu:
             e.tensor_tensor(o, i, j, a), reads=reads, writes=writes)

    def emit_rsqrt(u_ap, r_ap, rkeys, wkeys):
        u32v = lnu.ap().bitcast(U32)
        r32v = lnr.ap().bitcast(U32)
        p.op("vector", lambda e: e.tensor_scalar(
            r32v[:, :], u32v[:, :], 1, 0xFFFFFFFF, ALU.logical_shift_right, ALU.bitwise_xor),
            reads=rkeys, writes=wkeys)
        p.op("vector", lambda e: e.tensor_scalar(
            r32v[:, :], r32v[:, :], 0xFFFFFFFF - 0x5F3759DF, None, ALU.subtract),
            reads=wkeys, writes=wkeys)
        for _ in range(3):
            TT("vector", lnt[:, :], r_ap, r_ap, ALU.mult, wkeys, ["lnt"])
            TT("vector", lnt[:, :], lnt[:, :], u_ap, ALU.mult, ["lnt"] + list(rkeys), ["lnt"])
            TS("vector", lnt[:, :], lnt[:, :], -0.5, 1.5, ALU.mult, ALU.add, ["lnt"], ["lnt"])
            TT("vector", r_ap, r_ap, lnt[:, :], ALU.mult, wkeys + ["lnt"], wkeys)

    def STT(eng, out, in0, scal, in1, op0, op1, reads, writes):
        p.op(eng, lambda e, o=out, i=in0, s=scal, j=in1, u=op0, v=op1:
             e.scalar_tensor_tensor(o, i, s, j, u, v), reads=reads, writes=writes)

    def emit_ln_stats(x_ap, xkey):
        p.op("vector", lambda e: e.bn_stats(st[:, 0:6], x_ap[:, 0:512]),
             reads=[xkey], writes=["st"])
        p.op("vector", lambda e: e.bn_stats(st[:, 8:14], x_ap[:, 512:1024]),
             reads=[xkey], writes=["st"])
        ms = [st[:, 1:2], st[:, 4:5], st[:, 9:10], st[:, 12:13]]
        M2s = [st[:, 2:3], st[:, 5:6], st[:, 10:11], st[:, 13:14]]
        TT("vector", lnm[:, :], ms[0], ms[1], ALU.add, ["st"], ["lnm"])
        TT("vector", lnm[:, :], lnm[:, :], ms[2], ALU.add, ["lnm", "st"], ["lnm"])
        TT("vector", lnm[:, :], lnm[:, :], ms[3], ALU.add, ["lnm", "st"], ["lnm"])
        TS("vector", lnm[:, :], lnm[:, :], 0.25, None, ALU.mult, ALU.bypass, ["lnm"], ["lnm"])
        TT("vector", lnt[:, :], ms[0], ms[0], ALU.mult, ["st"], ["lnt"])
        for g in (1, 2, 3):
            STT("vector", lnt[:, :], ms[g], ms[g], lnt[:, :], ALU.mult, ALU.add,
                ["st", "lnt"], ["lnt"])
        TT("vector", lnu[:, :], M2s[0], M2s[1], ALU.add, ["st"], ["lnu"])
        TT("vector", lnu[:, :], lnu[:, :], M2s[2], ALU.add, ["lnu", "st"], ["lnu"])
        TT("vector", lnu[:, :], lnu[:, :], M2s[3], ALU.add, ["lnu", "st"], ["lnu"])
        TS("vector", lnu[:, :], lnu[:, :], 1.0 / 1024.0, None, ALU.mult, ALU.bypass,
           ["lnu"], ["lnu"])
        STT("vector", lnu[:, :], lnt[:, :], 0.25, lnu[:, :], ALU.mult, ALU.add,
            ["lnt", "lnu"], ["lnu"])
        TT("vector", lnt[:, :], lnm[:, :], lnm[:, :], ALU.mult, ["lnm"], ["lnt"])
        TT("vector", lnu[:, :], lnu[:, :], lnt[:, :], ALU.subtract, ["lnu", "lnt"], ["lnu"])
        TS("vector", lnu[:, :], lnu[:, :], EPS, None, ALU.add, ALU.bypass, ["lnu"], ["lnu"])
        emit_rsqrt(lnu[:, :], lnr[:, :], ["lnu"], ["lnr"])

    def dma(dst, src, reads, writes, eng="sync"):
        p.op(eng, lambda e, d=dst, s=src: e.dma_start(out=d, in_=s),
             reads=reads, writes=writes, dma=True)

    def mm(out, lhsT, rhs, start, stop, reads, writes):
        p.op("tensor", lambda e, o=out, l=lhsT, r=rhs, a=start, b=stop:
             e.matmul(o, l, r, start=a, stop=b), reads=reads, writes=writes)

    def transpose(out, in_, reads, writes):
        p.op("tensor", lambda e, o=out, i=in_: e.transpose(o, i, ident[:, :]),
             reads=list(reads) + ["ident"], writes=writes)

    def copy(eng, out, in_, reads, writes, scale=None):
        if scale is not None:
            p.op("scalar", lambda e, o=out, i=in_, s=scale: e.activation(
                o, i, AF.Copy, bias=0.0, scale=s), reads=reads, writes=writes)
        elif eng == "scalar":
            p.op(eng, lambda e, o=out, i=in_: e.copy(o, i),
                 reads=reads, writes=writes)
        else:
            p.op(eng, lambda e, o=out, i=in_: e.tensor_copy(o, i),
                 reads=reads, writes=writes)

    def maxfin(m_sb, mkey, cc_i, slot, pskey="ps0", psb=None):
        psb = ps[0] if psb is None else psb
        transpose(psb[0:1, 0:128], m_sb[:, :], [mkey], [pskey])
        p.op("vector", lambda e: e.tensor_reduce(
            sc_a[0:1, slot:slot + 1], psb[0:1, 0:128], AX.X, ALU.max),
            reads=[pskey], writes=["sc_a"])
        dma(cc_in[cc_i].ap()[0:1, slot:slot + 1], sc_a[0:1, slot:slot + 1],
            ["sc_a"], [f"cc{cc_i}i"])

    def emit_collective(cc_i):
        p.op("gpsimd", lambda e, i=cc_i: e.collective_compute(
            "AllReduce", ALU.max, replica_groups=[list(range(NCORES))],
            ins=[cc_in[i].ap().opt()], outs=[cc_out[i].ap().opt()]),
            reads=[f"cc{cc_i}i"], writes=[f"cc{cc_i}o"])

    def emit_scale(cc_i, slot, s_b, inv_b, skey, ikey, psb):
        dma(sc_b[0:1, slot:slot + 1], cc_out[cc_i].ap()[0:1, slot:slot + 1],
            [f"cc{cc_i}o"], ["sc_b"])
        TS("vector", sc_b[0:1, slot:slot + 1], sc_b[0:1, slot:slot + 1],
           1.0 / QMAX, 1e-12, ALU.mult, ALU.max, ["sc_b"], ["sc_b"])
        mm(psb[0:128, 0:1], onec[:, :], sc_b[0:1, slot:slot + 1], True, True,
           ["sc_b", "onec"], ["psc"])
        copy("vector", s_b[:, :], psb[0:128, 0:1], ["psc"], [skey])
        p.op("vector", lambda e, o=inv_b, i=s_b: e.reciprocal(o[:, :], i[:, :]),
             reads=[skey], writes=[ikey])

    class _Stop(Exception):
        pass

    def chk(tag):
        if stop == tag:
            raise _Stop()

    dma(ident[:, :], ident_h.ap(), [], ["ident"])
    dma(onec[:, :], ones_col_h.ap(), [], ["onec"])
    dma(one128[:, :], ones128_h.ap(), [], ["one128"])
    dma(bo_s[:, :], bo_h.ap(), [], ["bo"])
    for m, k in ((m_ln3, "m_ln3"), (m_up, "m_up"), (m_gate, "m_gate"),
                 (m_gs, "m_gs"), (m_o, "m_o")):
        p.op("vector", lambda e, t=m: e.memset(t[:, :], 0.0), reads=[], writes=[k])

    x2d = x_h.ap()
    x12d = x1_h.ap()
    out2d = out_h.ap()

    try:
        for s in range(BPC):
            p.barrier()
            for t in range(16):
                g = s * 16 + t
                dma(x_t[:, :], x2d[g * 128:(g + 1) * 128, :], [], ["x_t"])
                emit_ln_stats(x_t[:, :], "x_t")
                TS("vector", h_t[:, :], x_t[:, :], lnm[:, 0:1], lnr[:, 0:1],
                   ALU.subtract, ALU.mult, ["x_t", "lnm", "lnr"], ["h_t"])
                for c in range(8):
                    pk = f"ps{c % 2}"
                    transpose(ps[c % 2][:, 0:128], h_t[:, c * 128:(c + 1) * 128],
                              ["h_t"], [pk])
                    copy("scalar", hT(c)[:, t * 128:(t + 1) * 128], ps[c % 2][:, 0:128],
                         [pk], [f"hT{c}"])

            chk(f"A1s{s}")
            for pair in range(PAIRS):
                wk = "wsl"
                for c in range(8):
                    dma(wsl_[:, c * 128:(c + 1) * 128],
                        wq_h.ap()[c * 128:(c + 1) * 128, pair * 128:(pair + 1) * 128],
                        [], [wk])
                for c in range(8):
                    dma(wsl_[:, 1024 + c * 64:1024 + (c + 1) * 64],
                        wkv_h.ap()[c * 128:(c + 1) * 128, pair * 128:pair * 128 + 64],
                        [], [wk])
                    dma(wsl_[:, 1536 + c * 64:1536 + (c + 1) * 64],
                        wkv_h.ap()[c * 128:(c + 1) * 128, pair * 128 + 64:(pair + 1) * 128],
                        [], [wk])
                for c in range(8):
                    dma(wsl_[:, 2048 + c * 128:2048 + (c + 1) * 128],
                        wkv_h.ap()[c * 128:(c + 1) * 128, D + pair * 128:D + (pair + 1) * 128],
                        [], [wk])
                wq_sl = lambda c: wsl_[:, c * 128:(c + 1) * 128]
                wk_slA = lambda c: wsl_[:, 1024 + c * 64:1024 + (c + 1) * 64]
                wk_slB = lambda c: wsl_[:, 1536 + c * 64:1536 + (c + 1) * 64]
                wv_sl = lambda c: wsl_[:, 2048 + c * 128:2048 + (c + 1) * 128]

                for n in range(4):
                    pk = f"ps{n % 2}"
                    for c in range(8):
                        mm(ps[n % 2][:, :], wq_sl(c), hT(c)[:, n * 512:(n + 1) * 512],
                           c == 0, c == 7, [wk, f"hT{c}"], [pk])
                    copy("scalar", qT[:, n * 512:(n + 1) * 512], ps[n % 2][:, :],
                         [pk], ["qT"], scale=DH ** -0.5)
                for n in range(4):
                    pk = f"ps{n % 2}"
                    for c in range(8):
                        mm(ps[n % 2][0:64, :], wk_slA(c), hT(c)[:, n * 512:(n + 1) * 512],
                           c == 0, c == 7, [wk, f"hT{c}"], [pk])
                    for c in range(8):
                        mm(ps[n % 2][64:128, :], wk_slB(c), hT(c)[:, n * 512:(n + 1) * 512],
                           c == 0, c == 7, [wk, f"hT{c}"], [pk])
                    copy("scalar", kT[:, n * 512:(n + 1) * 512], ps[n % 2][:, :],
                         [pk], ["kT"])
                for t in range(16):
                    pk = f"ps{2 + t % 2}"
                    for c in range(8):
                        mm(ps[2 + t % 2][:, 0:128], hT(c)[:, t * 128:(t + 1) * 128],
                           wv_sl(c), c == 0, c == 7, [wk, f"hT{c}"], [pk])
                    copy("scalar", V[:, t * 128:(t + 1) * 128], ps[2 + t % 2][:, 0:128],
                         [pk], ["V"])

                for hh in range(2):
                    off = hh * 64
                    for qc in range(4):
                        qs = qT[off:off + 64, qc * 512:(qc + 1) * 512]
                        for kt in range(16):
                            pk = f"ps{kt % 2}"
                            ek = f"e{kt % 2}"
                            mm(ps[kt % 2][:, :], kT[off:off + 64, kt * 128:(kt + 1) * 128],
                               qs, True, True, ["kT", "qT"], [pk])
                            p.op("scalar", lambda e, o=expT[kt % 2][:, :], i=ps[kt % 2][:, :]:
                                 e.activation(o, i, AF.Exp), reads=[pk], writes=[ek])
                            mm(ps[4][0:64, :], V[:, kt * 128 + off:kt * 128 + off + 64],
                               expT[kt % 2][:, :], kt == 0, kt == 15, ["V", ek], ["ps4"])
                            mm(ps[5][0:1, :], one128[:, :], expT[kt % 2][:, :],
                               kt == 0, kt == 15, ["one128", ek], ["ps5"])
                        p.op("vector", lambda e: e.reciprocal(r_row[0:1, :], ps[5][0:1, :]),
                             reads=["ps5"], writes=["r_row"])
                        mm(ps[6][:, :], onec[:, :], r_row[0:1, :], True, True,
                           ["onec", "r_row"], ["ps6"])
                        copy("scalar", Rb[:, :], ps[6][:, :], ["ps6"], ["Rb"])
                        TT("vector", oT(pair)[off:off + 64, qc * 512:(qc + 1) * 512],
                           ps[4][0:64, :], Rb[0:64, :], ALU.mult,
                           ["ps4", "Rb"], [f"oT{pair}"])
                chk(f"A2s{s}p{pair}")

            p.barrier()
            for c in range(8):
                dma(WoV[:, c * 1024:(c + 1) * 1024],
                    wo_h.ap()[c * 128:(c + 1) * 128, :], [], ["Wo"])
            for t in range(16):
                g = s * 16 + t
                for half in range(2):
                    pk = f"ps{6 + half}"
                    for c in range(8):
                        mm(ps[6 + half][:, :], oT(c)[:, t * 128:(t + 1) * 128],
                           WoV[:, c * 1024 + half * 512:c * 1024 + (half + 1) * 512],
                           c == 0, False, [f"oT{c}", "Wo"], [pk])
                    mm(ps[6 + half][:, :], onec[:, :], bo_s[0:1, half * 512:(half + 1) * 512],
                       False, True, ["onec", "bo"], [pk])
                    copy("scalar", h_t[:, half * 512:(half + 1) * 512], ps[6 + half][:, :],
                         [pk], ["h_t"])
                emit_ln_stats(h_t[:, :], "h_t")
                dma(x_t[:, :], x2d[g * 128:(g + 1) * 128, :], [], ["x_t"])
                TS("vector", h_t[:, :], h_t[:, :], lnm[:, 0:1], lnr[:, 0:1],
                   ALU.subtract, ALU.mult, ["h_t", "lnm", "lnr"], ["h_t"])
                TT("vector", x1_t[:, :], x_t[:, :], h_t[:, :], ALU.add,
                   ["x_t", "h_t"], ["x1_t"])
                dma(x12d[g * 128:(g + 1) * 128, :], x1_t[:, :], ["x1_t"], ["x1_hbm"])
                emit_ln_stats(x1_t[:, :], "x1_t")
                TS("vector", ln3_t[:, :], x1_t[:, :], lnm[:, 0:1], lnr[:, 0:1],
                   ALU.subtract, ALU.mult, ["x1_t", "lnm", "lnr"], ["ln3_t"])
                p.op("vector", lambda e: e.tensor_reduce(
                    amax_t[:, :], ln3_t[:, :], AX.X, ALU.max, apply_absolute_value=True),
                    reads=["ln3_t"], writes=["amax_t"])
                TT("vector", m_ln3[:, :], m_ln3[:, :], amax_t[:, :], ALU.max,
                   ["m_ln3", "amax_t"], ["m_ln3"])
                for c in range(8):
                    pk = f"ps{c % 2}"
                    transpose(ps[c % 2][:, 0:128], ln3_t[:, c * 128:(c + 1) * 128],
                              ["ln3_t"], [pk])
                    copy("scalar", stage[:, c * 128:(c + 1) * 128], ps[c % 2][:, 0:128],
                         [pk], ["stage"])
                dma(ln3_h.ap()[g].rearrange("c p j -> p c j"),
                    stage.ap().rearrange("p (c j) -> p c j", j=128),
                    ["stage"], ["ln3_hbm"])

        chk("A")
        p.barrier()
        maxfin(m_ln3, "m_ln3", 0, 0)
        emit_collective(0)
        emit_scale(0, 0, sxb, invxb, "sxb", "invxb", ps[1])

        p.barrier()
        for c in range(8):
            xq = xqk(c)
            dma(xq.rearrange("p (t j) -> p t j", j=128),
                ln3_h.ap()[:, c].rearrange("t p j -> p t j"), ["ln3_hbm"], [f"xq{c}"])
            TS("vector", xq, xq, invxb[:, 0:1], MAGIC, ALU.mult, ALU.add,
               [f"xq{c}", "invxb"], [f"xq{c}"])
            TS("vector", xq, xq, MAGIC, None, ALU.subtract, ALU.bypass,
               [f"xq{c}"], [f"xq{c}"])

        for wname, w_hnd, spill, mkey, m_sb in (
                ("up", wup_h, up_h, "m_up", m_up),
                ("gate", wgate_h, gate_h, "m_gate", m_gate)):
            for i in range(IC):
                wn = wnat[:, 0:1024]
                dma(wn, w_hnd.ap()[i * 128:(i + 1) * 128, :], [], ["wnat"])
                w3 = wnat[:, 0:1024].rearrange("p (b k) -> p b k", k=32)
                p.op("vector", lambda e, w=w3: e.tensor_reduce(
                    wsc[:, 0:32], w, AX.X, ALU.max, apply_absolute_value=True),
                    reads=["wnat"], writes=["wsc"])
                TS("vector", wsc[:, 0:32], wsc[:, 0:32], 1.0 / 7.0, 1e-12,
                   ALU.mult, ALU.max, ["wsc"], ["wsc"])
                p.op("vector", lambda e: e.reciprocal(wrc[:, 0:32], wsc[:, 0:32]),
                     reads=["wsc"], writes=["wrc"])
                rb = wrc.ap()[:, 0:32].rearrange("p (b o) -> p b o", o=1).broadcast_to([128, 32, 32])
                sbc = wsc.ap()[:, 0:32].rearrange("p (b o) -> p b o", o=1).broadcast_to([128, 32, 32])
                TT("vector", w3, w3, rb, ALU.mult, ["wnat", "wrc"], ["wnat"])
                TS("vector", wn, wn, MAGIC, MAGIC, ALU.add, ALU.subtract, ["wnat"], ["wnat"])
                TT("vector", w3, w3, sbc, ALU.mult, ["wnat", "wsc"], ["wnat"])
                for c in range(8):
                    pk = f"ps{c % 2}"
                    transpose(ps[c % 2][:, 0:128], wn[:, c * 128:(c + 1) * 128],
                              ["wnat"], [pk])
                    copy("scalar", wT[:, c * 128:(c + 1) * 128], ps[c % 2][:, 0:128],
                         [pk], ["wT"])
                for n in range(8):
                    pk = f"ps{6 + n % 2}"
                    for c in range(8):
                        mm(ps[6 + n % 2][:, :], wT[:, c * 128:(c + 1) * 128],
                           xqk(c)[:, n * 512:(n + 1) * 512], c == 0, c == 7,
                           ["wT", f"xq{c}"], [pk])
                    TS("vector", upr[:, n * 512:(n + 1) * 512], ps[6 + n % 2][:, :],
                       sxb[:, 0:1], None, ALU.mult, ALU.bypass, [pk, "sxb"], ["upr"])
                p.op("vector", lambda e: e.tensor_reduce(
                    amax_t[:, :], upr[:, :], AX.X, ALU.max, apply_absolute_value=True),
                    reads=["upr"], writes=["amax_t"])
                TT("vector", m_sb[:, :], m_sb[:, :], amax_t[:, :], ALU.max,
                   [mkey, "amax_t"], [mkey])
                dma(spill.ap()[i], upr[:, :], ["upr"], [f"{wname}_hbm{i}"])

        chk("C")
        p.barrier()
        maxfin(m_up, "m_up", 1, 0)
        maxfin(m_gate, "m_gate", 1, 1, pskey="ps1", psb=ps[1])
        emit_collective(1)
        emit_scale(1, 0, sub_, invub, "sub", "invub", ps[2])
        emit_scale(1, 1, sgb, invgb, "sgb", "invgb", ps[3])

        p.barrier()
        for i in range(IC):
            dma(rowA, gate_h.ap()[i], [f"gate_hbm{i}"], ["rowA"])
            TS("vector", rowA, rowA, invgb[:, 0:1], MAGIC, ALU.mult, ALU.add,
               ["rowA", "invgb"], ["rowA"])
            TS("vector", rowA, rowA, MAGIC, None, ALU.subtract, ALU.bypass,
               ["rowA"], ["rowA"])
            p.op("scalar", lambda e: e.activation(rowB, rowA, AF.Sigmoid,
                 bias=0.0, scale=sgb[:, 0:1]), reads=["rowA", "sgb"], writes=["rowB"])
            TS("vector", rowB, rowB, 65536.0, MAGIC, ALU.mult, ALU.add, ["rowB"], ["rowB"])
            TS("vector", rowB, rowB, MAGIC, 65535.0, ALU.subtract, ALU.min, ["rowB"], ["rowB"])
            TS("vector", rowA, rowA, sgb[:, 0:1], None, ALU.mult, ALU.bypass,
               ["rowA", "sgb"], ["rowA"])
            p.op("vector", lambda e: e.scalar_tensor_tensor(
                rowA, rowB, 1.0 / 65536.0, rowA, ALU.mult, ALU.mult),
                reads=["rowA", "rowB"], writes=["rowA"])
            p.op("vector", lambda e: e.tensor_reduce(
                amax_t[:, :], rowA, AX.X, ALU.max, apply_absolute_value=True),
                reads=["rowA"], writes=["amax_t"])
            TT("vector", m_gs[:, :], m_gs[:, :], amax_t[:, :], ALU.max,
               ["m_gs", "amax_t"], ["m_gs"])
            dma(gate_h.ap()[i], rowA, ["rowA"], [f"gate_hbm{i}"])

        chk("E")
        p.barrier()
        maxfin(m_gs, "m_gs", 2, 0)
        emit_collective(2)
        emit_scale(2, 0, sgsb, invgsb, "sgsb", "invgsb", ps[1])

        p.barrier()
        for i in range(IC):
            dma(rowA, gate_h.ap()[i], [f"gate_hbm{i}"], ["rowA"])
            dma(rowB, up_h.ap()[i], [f"up_hbm{i}"], ["rowB"])
            TS("vector", rowA, rowA, invgsb[:, 0:1], MAGIC, ALU.mult, ALU.add,
               ["rowA", "invgsb"], ["rowA"])
            TS("vector", rowA, rowA, MAGIC, None, ALU.subtract, ALU.bypass,
               ["rowA"], ["rowA"])
            TS("vector", rowA, rowA, sgsb[:, 0:1], None, ALU.mult, ALU.bypass,
               ["rowA", "sgsb"], ["rowA"])
            TS("vector", rowB, rowB, invub[:, 0:1], MAGIC, ALU.mult, ALU.add,
               ["rowB", "invub"], ["rowB"])
            TS("vector", rowB, rowB, MAGIC, None, ALU.subtract, ALU.bypass,
               ["rowB"], ["rowB"])
            p.op("vector", lambda e: e.scalar_tensor_tensor(
                rowA, rowB, sub_[:, 0:1], rowA, ALU.mult, ALU.mult),
                reads=["rowA", "rowB", "sub"], writes=["rowA"])
            p.op("vector", lambda e: e.tensor_reduce(
                amax_t[:, :], rowA, AX.X, ALU.max, apply_absolute_value=True),
                reads=["rowA"], writes=["amax_t"])
            TT("vector", m_o[:, :], m_o[:, :], amax_t[:, :], ALU.max,
               ["m_o", "amax_t"], ["m_o"])
            dma(up_h.ap()[i], rowA, ["rowA"], [f"up_hbm{i}"])

        chk("G")
        p.barrier()
        maxfin(m_o, "m_o", 3, 0)
        emit_collective(3)
        emit_scale(3, 0, sob, invob, "sob", "invob", ps[1])

        p.barrier()
        for dt in range(8):
            wn3 = wnat[:, 0:3072]
            dma(wn3, wdown_h.ap()[dt * 128:(dt + 1) * 128, :], [], ["wnat"])
            w3 = wnat[:, 0:3072].rearrange("p (b k) -> p b k", k=32)
            p.op("vector", lambda e, w=w3: e.tensor_reduce(
                wsc[:, 0:96], w, AX.X, ALU.max, apply_absolute_value=True),
                reads=["wnat"], writes=["wsc"])
            TS("vector", wsc[:, 0:96], wsc[:, 0:96], 1.0 / 7.0, 1e-12,
               ALU.mult, ALU.max, ["wsc"], ["wsc"])
            p.op("vector", lambda e: e.reciprocal(wrc[:, 0:96], wsc[:, 0:96]),
                 reads=["wsc"], writes=["wrc"])
            rb = wrc.ap()[:, 0:96].rearrange("p (b o) -> p b o", o=1).broadcast_to([128, 96, 32])
            sbc = wsc.ap()[:, 0:96].rearrange("p (b o) -> p b o", o=1).broadcast_to([128, 96, 32])
            TT("vector", w3, w3, rb, ALU.mult, ["wnat", "wrc"], ["wnat"])
            TS("vector", wn3, wn3, MAGIC, MAGIC, ALU.add, ALU.subtract, ["wnat"], ["wnat"])
            TT("vector", w3, w3, sbc, ALU.mult, ["wnat", "wsc"], ["wnat"])
            for i in range(IC):
                pk = f"ps{i % 2}"
                transpose(ps[i % 2][:, 0:128], wn3[:, i * 128:(i + 1) * 128],
                          ["wnat"], [pk])
                copy("scalar", wdT(i)[:, dt * 128:(dt + 1) * 128], ps[i % 2][:, 0:128],
                     [pk], [f"wdT{i}"])

        p.barrier()
        for tg in range(8):
            for i in range(IC):
                kb = kob[i % 2]
                kk = f"ko{i % 2}"
                dma(kb, up_h.ap()[i, :, tg * 512:(tg + 1) * 512], [f"up_hbm{i}"], [kk])
                TS("vector", kb, kb, invob[:, 0:1], MAGIC, ALU.mult, ALU.add,
                   [kk, "invob"], [kk])
                TS("vector", kb, kb, MAGIC, None, ALU.subtract, ALU.bypass, [kk], [kk])
                for t4 in range(4):
                    for half in range(2):
                        pk = f"ps{t4 * 2 + half}"
                        mm(ps[t4 * 2 + half][:, :], kb[:, t4 * 128:(t4 + 1) * 128],
                           wdT(i)[:, half * 512:(half + 1) * 512], i == 0, i == IC - 1,
                           [kk, f"wdT{i}"], [pk])
            for t4 in range(4):
                g = tg * 4 + t4
                for half in range(2):
                    pk = f"ps{t4 * 2 + half}"
                    p.op("scalar", lambda e, o=ffn_t[:, half * 512:(half + 1) * 512],
                         i=ps[t4 * 2 + half][:, :]: e.activation(
                             o, i, AF.Copy, bias=0.0, scale=sob[:, 0:1]),
                         reads=[pk, "sob"], writes=["ffn_t"])
                emit_ln_stats(ffn_t[:, :], "ffn_t")
                dma(x1_t[:, :], x12d[g * 128:(g + 1) * 128, :], ["x1_hbm"], ["x1_t"])
                TS("vector", ffn_t[:, :], ffn_t[:, :], lnm[:, 0:1], lnr[:, 0:1],
                   ALU.subtract, ALU.mult, ["ffn_t", "lnm", "lnr"], ["ffn_t"])
                TT("vector", outt[:, :], x1_t[:, :], ffn_t[:, :], ALU.add,
                   ["x1_t", "ffn_t"], ["outt"])
                dma(out2d[g * 128:(g + 1) * 128, :], outt[:, :], ["outt"], ["out_hbm"])


    except _Stop:
        pass

    p.barrier()

    for e in ENGINES:
        assert p.cnt[e] < 60000, (e, p.cnt[e])

    with nc.Block() as block:
        block.tensor(lambda obj: [a(obj) for a in p.actions["tensor"]])
        block.vector(lambda obj: [a(obj) for a in p.actions["vector"]])
        block.scalar(lambda obj: [a(obj) for a in p.actions["scalar"]])
        block.gpsimd(lambda obj: [a(obj) for a in p.actions["gpsimd"]])
        block.sync(lambda obj: [a(obj) for a in p.actions["sync"]])

    ctx.close()
    return nc, p.n_instr


def kernel(L0_attn_hidden_states, L0_attn_Wq, L0_attn_Wkv, L0_attn_Wo,
           L0_attn_bo, L0_ffn_w_gate, L0_ffn_w_up, L0_ffn_w_down):
    hs = np.ascontiguousarray(L0_attn_hidden_states, dtype=np.float32)
    wq = np.ascontiguousarray(L0_attn_Wq, dtype=np.float32)
    wkv = np.ascontiguousarray(L0_attn_Wkv, dtype=np.float32)
    wo = np.ascontiguousarray(L0_attn_Wo, dtype=np.float32)
    bo = np.ascontiguousarray(L0_attn_bo, dtype=np.float32).reshape(1, D)
    wg = np.ascontiguousarray(L0_ffn_w_gate, dtype=np.float32)
    wu = np.ascontiguousarray(L0_ffn_w_up, dtype=np.float32)
    wd = np.ascontiguousarray(L0_ffn_w_down, dtype=np.float32)

    nc, n_instr = build_nc()
    in_maps = []
    for c in range(NCORES):
        in_maps.append({
            "x": hs[c * BPC:(c + 1) * BPC].reshape(TPC, D),
            "wq": wq, "wkv": wkv, "wo": wo, "bo": bo,
            "wup": wu, "wgate": wg, "wdown": wd,
        })
    res = run_bass_kernel_spmd(nc, in_maps, core_ids=list(range(NCORES)))
    out = np.stack([res.results[c]["out"].reshape(BPC, S, D)
                    for c in range(NCORES)], axis=0)
    return np.ascontiguousarray(out.reshape(NCORES * BPC, S, D))
